# version 03 (final)
# baseline (speedup 1.0000x reference)
import contextlib
import numpy as np
import ml_dtypes

import concourse.bass as bass
import concourse.mybir as mybir
from concourse.bass_utils import run_bass_kernel_spmd

F32 = mybir.dt.float32
BF16 = mybir.dt.bfloat16
AF = mybir.ActivationFunctionType
ALU = mybir.AluOpType
AX = mybir.AxisListType.X

D = 2048
NF = 16
TS = 512
TPC = 4096
NTC = TPC // TS
E = 8
IM = 1408
NI = 11
ISH = 5632
NU = 12
SEQ = 2048
G = 8
HD = 64
EPS = 1e-5
NEGC = -3.0
RBIG = -1e30

ENGINES = ("sync", "tensor", "scalar", "vector", "gpsimd")
Q2E = {"qs": "sync", "qa": "scalar", "qg": "gpsimd"}


class ESem:
    __slots__ = ("name", "n", "h")

    def __init__(self, name):
        self.name = name
        self.n = 0
        self.h = None

    def inc(self, amt=1):
        self.n += amt
        return self.n


class Op:
    __slots__ = ("fn", "waits", "inc", "lowp")

    def __init__(self, fn, waits, inc, lowp=False):
        self.fn = fn
        self.waits = waits
        self.inc = inc
        self.lowp = lowp


def compact(waits):
    best = {}
    for (s, v) in waits:
        if v > best.get(s, 0):
            best[s] = v
    return [(s, v) for s, v in best.items()]


class Res:
    __slots__ = ("w", "rs")

    def __init__(self):
        self.w = None
        self.rs = []

    def rd_waits(self):
        return [self.w] if self.w else []

    def rd_set(self, ms):
        self.rs.append(ms)

    def wr_waits(self):
        ws = list(self.rs)
        if self.w:
            ws.append(self.w)
        return ws

    def wr_set(self, ms):
        self.w = ms
        self.rs = []


class Buf:
    __slots__ = ("t", "res", "dsem")

    def __init__(self, t):
        self.t = t
        self.res = Res()
        self.dsem = None

    def __getitem__(self, sl):
        return self.t[sl]


class B:
    def __init__(self, nc):
        self.nc = nc
        self.ops = {e: [] for e in ENGINES}
        self.pe = ESem("s_pe")
        self.dv = ESem("s_dv")
        self.ac = ESem("s_ac")
        self.gp = ESem("s_gp")
        self.qs = ESem("s_qs")
        self.qa = ESem("s_qa")
        self.qg = ESem("s_qg")
        self.all_sems = [self.pe, self.dv, self.ac, self.gp,
                         self.qs, self.qa, self.qg]
        self.barrier_pending = {e: None for e in ENGINES}

    _uid = 0

    def mksem(self, name):
        B._uid += 1
        s = ESem(f"{name}_{B._uid}")
        self.all_sems.append(s)
        return s

    def emit(self, engine, fn, waits=(), inc=None, lowp=False):
        bw = self.barrier_pending[engine]
        if bw is not None:
            waits = list(waits) + bw
            self.barrier_pending[engine] = None
        self.ops[engine].append(Op(fn, compact(waits), inc, lowp))

    def barrier(self):
        snap = [(s, s.n) for s in self.all_sems if s.n > 0]
        for e in ENGINES:
            self.barrier_pending[e] = list(snap)

    def realize(self, es):
        for s in self.all_sems:
            s.h = es.enter_context(self.nc.semaphore(s.name))
        block = es.enter_context(self.nc.Block())
        for ename in ENGINES:
            ops = self.ops[ename]
            if not ops:
                continue

            def make(ops=ops):
                def emitter(eng):
                    seen = {}
                    for o in ops:
                        for (s, v) in o.waits:
                            if v > 0 and seen.get(s, 0) < v:
                                eng.wait_ge(s.h, v)
                                seen[s] = v
                        if o.lowp:
                            with self.nc.allow_low_precision("deliberate bf16"):
                                ins = o.fn(eng)
                        else:
                            ins = o.fn(eng)
                        if o.inc is not None:
                            s, amt = o.inc
                            ins.then_inc(s.h, amt)
                return emitter
            getattr(block, ename)(make())

    def dma(self, q, out_ap, in_ap, writes=(), reads=(), extra=(), sem=None, **kw):
        if sem is None:
            sem = getattr(self, q)
        ms = (sem, sem.inc(16))
        waits = list(extra)
        for r in writes:
            waits += r.wr_waits()
        for r in reads:
            waits += r.rd_waits()
        self.emit(Q2E[q],
                  lambda e, o=out_ap, i=in_ap, kw=kw: e.dma_start(out=o, in_=i, **kw),
                  waits, (sem, 16))
        for r in writes:
            r.wr_set(ms)
        for r in reads:
            r.rd_set(ms)
        return ms

    def _cop(self, engine, sem, fn, writes, reads, extra, lowp=False):
        ms = (sem, sem.inc(1))
        waits = list(extra)
        for r in writes:
            waits += r.wr_waits()
        for r in reads:
            waits += r.rd_waits()
        self.emit(engine, fn, waits, (sem, 1), lowp=lowp)
        for r in writes:
            r.wr_set(ms)
        for r in reads:
            r.rd_set(ms)
        return ms

    def vec(self, fn, writes=(), reads=(), extra=(), lowp=False):
        return self._cop("vector", self.dv, fn, writes, reads, extra, lowp=lowp)

    def act(self, fn, writes=(), reads=(), extra=()):
        return self._cop("scalar", self.ac, fn, writes, reads, extra)

    def mm_group(self, bank, out_ap, mms, reads=(), extra=(), tile_position=None):
        ms = (self.pe, self.pe.inc(1))
        waits = list(extra) + bank.res.wr_waits()
        for r in reads:
            waits += r.rd_waits()
        n = len(mms)
        for i, (lh, rh) in enumerate(mms):
            self.emit(
                "tensor",
                (lambda e, o=out_ap, l=lh, r=rh, st=(i == 0), sp=(i == n - 1),
                        tp=tile_position:
                 e.matmul(o, l, r, start=st, stop=sp, tile_position=tp)),
                waits if i == 0 else (),
                (self.pe, 1) if i == n - 1 else None)
        bank.res.wr_set(ms)
        for r in reads:
            r.rd_set(ms)
        return ms

    def transpose(self, bank, out_ap, in_ap, ident_ap, writes=(), reads=(), extra=()):
        ms = (self.pe, self.pe.inc(1))
        waits = list(extra) + bank.res.wr_waits()
        for r in writes:
            waits += r.wr_waits()
        for r in reads:
            waits += r.rd_waits()
        self.emit("tensor",
                  lambda e, o=out_ap, i=in_ap, id_=ident_ap: e.transpose(o, i, id_),
                  waits, (self.pe, 1))
        bank.res.wr_set(ms)
        for r in writes:
            r.wr_set(ms)
        for r in reads:
            r.rd_set(ms)
        return ms


class NS:
    pass


def declare_drams(nc):
    d = NS()
    dt = nc.dram_tensor
    d.x_fm = dt("x_fm", [D, TPC], F32, kind="ExternalInput").ap()
    d.wq = dt("wq", [D, D], F32, kind="ExternalInput").ap()
    d.wk = dt("wk", [512, D], F32, kind="ExternalInput").ap()
    d.wv = dt("wv", [512, D], F32, kind="ExternalInput").ap()
    d.wo = dt("wo", [D, D], F32, kind="ExternalInput").ap()
    d.wr = dt("wr", [8, D], F32, kind="ExternalInput").ap()
    d.wsg = dt("wsg", [1, D], F32, kind="ExternalInput").ap()
    d.gup = dt("gup", [E, 2 * IM, D], F32, kind="ExternalInput").ap()
    d.dnp = dt("dnp", [E, D, IM], F32, kind="ExternalInput").ap()
    d.shg = dt("shg", [ISH, D], F32, kind="ExternalInput").ap()
    d.shu = dt("shu", [ISH, D], F32, kind="ExternalInput").ap()
    d.shd = dt("shd", [D, ISH], F32, kind="ExternalInput").ap()
    d.c_ident = dt("c_ident", [128, 128], BF16, kind="ExternalInput").ap()
    d.c_identf = dt("c_identf", [128, 128], F32, kind="ExternalInput").ap()
    d.c_ones128b = dt("c_ones128b", [128, 1], BF16, kind="ExternalInput").ap()
    d.c_ones128f = dt("c_ones128f", [128, 1], F32, kind="ExternalInput").ap()
    d.c_onesrowf = dt("c_onesrowf", [65, 128], F32, kind="ExternalInput").ap()
    d.c_onesrowb = dt("c_onesrowb", [65, 128], BF16, kind="ExternalInput").ap()
    d.c_selb = dt("c_selb", [9, 9 * 128], BF16, kind="ExternalInput").ap()
    d.c_maskT = dt("c_maskT", [128, 896], BF16, kind="ExternalInput").ap()
    d.sgu = dt("sgu", [NU, 2 * IM, D], BF16, kind="Internal").ap()
    d.sdn = dt("sdn", [NU, D, IM], BF16, kind="Internal").ap()
    d.wq_bf = dt("wq_bf", [D, D], BF16, kind="Internal").ap()
    d.wk_bf = dt("wk_bf", [512, D], BF16, kind="Internal").ap()
    d.wv_bf = dt("wv_bf", [512, D], BF16, kind="Internal").ap()
    d.wo_bf = dt("wo_bf", [D, D], BF16, kind="Internal").ap()
    d.dbg_xln = dt("dbg_xln", [D, TPC], BF16, kind="ExternalOutput").ap()
    d.dbg_cgt = dt("dbg_cgt", [9, TPC], BF16, kind="ExternalOutput").ap()
    d.wtt_gu = dt("wtt_gu", [NU, 2 * IM * D], BF16, kind="Internal").ap()
    d.wtt_dn = dt("wtt_dn", [NU, D * IM], BF16, kind="Internal").ap()
    d.wtt_wq = dt("wtt_wq", [D * D], BF16, kind="Internal").ap()
    d.wtt_wk = dt("wtt_wk", [512 * D], BF16, kind="Internal").ap()
    d.wtt_wv = dt("wtt_wv", [512 * D], BF16, kind="Internal").ap()
    d.wtt_wo = dt("wtt_wo", [D * D], BF16, kind="Internal").ap()
    d.y1 = dt("y1_fm", [D, TPC], F32, kind="Internal").ap()
    d.y2 = dt("y2_fm", [D, TPC], F32, kind="Internal").ap()
    d.xln_fm = dt("xln_fm", [D, TPC], BF16, kind="Internal").ap()
    d.out = dt("out_fm", [D, TPC], F32, kind="ExternalOutput").ap()
    return d


def alloc_persistent(b, es):
    nc = b.nc
    p = NS()

    def sb(name, shape, dt):
        return Buf(es.enter_context(nc.sbuf_tensor(name, shape, dt)))
    p.ident = sb("ident", [128, 128], BF16)
    p.identf = sb("identf", [128, 128], F32)
    p.ones128b = sb("ones128b", [128, 1], BF16)
    p.ones128f = sb("ones128f", [128, 1], F32)
    p.onesrowf = sb("onesrowf", [65, 128], F32)
    p.onesrowb = sb("onesrowb", [65, 128], BF16)
    p.selb = sb("selb", [9, 9 * 128], BF16)
    p.maskT = sb("maskT", [128, 896], BF16)
    p.wr9 = sb("wr9", [9, D], F32)
    p.wrT = sb("wrT", [128, NF * 9], F32)
    p.rb = sb("rb", [128, 9], F32)
    p.eps_t = sb("eps_t", [1, 1], F32)
    p.negC = sb("negC", [128, 1], F32)
    for nm in ("negm", "msq", "var_r", "rstd", "sneg"):
        setattr(p, nm, sb(nm, [1, TS], F32))
    p.mr = sb("mr", [128, 8], F32)
    p.rstd_b = sb("rstd_b", [128, TS], F32)
    p.sneg_b = sb("sneg_b", [128, TS], F32)
    for nm in ("t9", "l9", "eq8", "t28", "m1", "nm1", "m2", "dex",
               "den", "rden", "keep", "t38"):
        w = 9 if nm in ("t9", "l9") else (8 if nm.endswith("8") or nm == "keep" else 1)
        setattr(p, nm, sb(nm, [128, w], F32))
    p.cg = sb("cg", [128, 9], F32)
    p.cgT = sb("cgT", [9, 4 * 128], BF16)
    p.xst = [sb("xst0", [128, TS], F32), sb("xst1", [128, TS], F32)]
    p.yst = [sb("yst0", [128, TS], F32), sb("yst1", [128, TS], F32)]
    for i in (0, 1):
        p.xst[i].dsem = b.mksem(f"d_xst{i}")
        p.yst[i].dsem = b.mksem(f"d_yst{i}")
    p.csem = b.mksem("d_consts")
    p.dbgsem = b.mksem("d_dbg")
    p.bounce = [sb("bounce0", [128, D], BF16), sb("bounce1", [128, D], BF16)]
    for i in (0, 1):
        p.bounce[i].dsem = b.mksem(f"d_bnc{i}")
    p.pb = [Buf(es.enter_context(nc.psum_tensor(f"pb{i}", [128, 512], F32)))
            for i in range(7)]
    p.pbT = Buf(es.enter_context(nc.psum_tensor("pbT", [128, 1024], BF16)))
    return p


def emit_preprocess(b, d, p, cfg):
    cbufs = [(p.ident, d.c_ident), (p.identf, d.c_identf),
             (p.ones128b, d.c_ones128b), (p.ones128f, d.c_ones128f),
             (p.onesrowf, d.c_onesrowf), (p.onesrowb, d.c_onesrowb),
             (p.selb, d.c_selb), (p.maskT, d.c_maskT)]
    for (buf, src) in cbufs:
        b.dma("qs", buf[:, :], src, writes=[buf.res], sem=p.csem)
    b.dma("qs", p.wr9[0:8, :], d.wr, writes=[p.wr9.res], sem=p.csem)
    b.dma("qs", p.wr9[8:9, :], d.wsg, writes=[p.wr9.res], sem=p.csem)
    cms = (p.csem, p.csem.n)
    for (buf, _) in cbufs:
        buf.res.w = cms
    p.wr9.res.w = cms
    b.vec(lambda e: e.memset(p.eps_t[:, :], EPS), writes=[p.eps_t.res])
    b.vec(lambda e: e.memset(p.negC[:, :], NEGC), writes=[p.negC.res])
    for k in range(NF):
        bank = p.pb[k % 2]
        b.transpose(bank, bank[0:128, 0:9], p.wr9[0:9, k * 128:(k + 1) * 128],
                    p.identf[0:9, 0:9], reads=[p.wr9.res, p.identf.res])
        b.vec(lambda e, k=k, bank=bank: e.tensor_copy(
            p.wrT[:, k * 9:(k + 1) * 9], bank[0:128, 0:9]),
            writes=[p.wrT.res], reads=[bank.res])
    bank = p.pb[2]
    b.mm_group(bank, bank[0:1, 0:9],
               [(p.ones128f[:, :], p.wrT[:, k * 9:(k + 1) * 9]) for k in range(NF)],
               reads=[p.ones128f.res, p.wrT.res])
    rrow = p.var_r
    b.vec(lambda e: e.tensor_copy(rrow[0:1, 0:9], bank[0:1, 0:9]),
          writes=[rrow.res], reads=[bank.res])
    bank2 = p.pb[3]
    b.mm_group(bank2, bank2[0:128, 0:9], [(p.onesrowf[0:1, :], rrow[0:1, 0:9])],
               reads=[p.onesrowf.res, rrow.res])
    b.vec(lambda e: e.tensor_copy(p.rb[:, :], bank2[0:128, 0:9]),
          writes=[p.rb.res], reads=[bank2.res])

    d.rrb = 0
    d.wtt_sem = [b.mksem(f"d_wtt{u}") for u in range(NU)]
    d.wtt_res = [Res() for _ in range(NU)]
    d.wtta_sem = {nm: b.mksem(f"d_wtta_{nm}") for nm in ("wq", "wk", "wv", "wo")}
    d.wtta_res = {nm: Res() for nm in ("wq", "wk", "wv", "wo")}
    d.cast_res = [Res() for _ in range(NU)]
    d.castd_res = [Res() for _ in range(NU)]
    d.casta_res = {}
    for u in range(NU):
        if u < E:
            gsrc = d.gup[u][0:IM, :].rearrange("(j q) c -> j q c", q=128)
            usrc = d.gup[u][IM:2 * IM, :].rearrange("(j q) c -> j q c", q=128)
        else:
            pc = u - E
            gsrc = d.shg[pc * NI * 128:(pc + 1) * NI * 128, :] \
                .rearrange("(j q) c -> j q c", q=128)
            usrc = d.shu[pc * NI * 128:(pc + 1) * NI * 128, :] \
                .rearrange("(j q) c -> j q c", q=128)
        dst = d.sgu[u].rearrange("(j s q) c -> s j q c", s=2, j=NI, q=128)
        sg_sem = b.mksem(f"d_cg{u}")
        b.dma("qg", dst[0], gsrc, sem=sg_sem)
        b.dma("qg", dst[1], usrc, sem=sg_sem)
        d.cast_res[u].w = (sg_sem, sg_sem.n)
        sd_sem = b.mksem(f"d_cd{u}")
        if u < E:
            b.dma("qg", d.sdn[u], d.dnp[u], sem=sd_sem)
        else:
            pc = u - E
            b.dma("qg", d.sdn[u], d.shd[:, pc * IM:(pc + 1) * IM], sem=sd_sem)
        d.castd_res[u].w = (sd_sem, sd_sem.n)
    for nm, wsrc, dst in (("wk", d.wk, d.wk_bf),
                          ("wv", d.wv, d.wv_bf), ("wo", d.wo, d.wo_bf)):
        r = Res()
        ca_sem = b.mksem(f"d_ca_{nm}")
        b.dma("qg", dst, wsrc, sem=ca_sem)
        r.w = (ca_sem, ca_sem.n)
        d.casta_res[nm] = r
    r = Res()
    ca_sem = b.mksem("d_ca_wq")
    for a in range(4):
        for j in range(4):
            wsrc = d.wq[a * 512:(a + 1) * 512, :].rearrange(
                "(h j x) c -> j h x c", h=2, j=4, x=64)[j]
            dst = d.wq_bf[a * 512 + j * 128:a * 512 + (j + 1) * 128, :]
            b.dma("qg", dst, wsrc, sem=ca_sem)
    r.w = (ca_sem, ca_sem.n)
    d.casta_res["wq"] = r


def emit_pre_unit(b, d, p, u):
    wsem = d.wtt_sem[u]
    rgs = [(0, 512), (512, 512), (1024, 384)]
    for k in range(NF):
        for half in range(2):
            h0 = half * IM
            bn = p.bounce[d.rrb % 2]
            d.rrb += 1
            b.dma("qs", bn[:, 0:IM],
                  d.sgu[u][h0:h0 + IM, k * 128:(k + 1) * 128],
                  writes=[bn.res], reads=[d.cast_res[u]], sem=bn.dsem,
                  transpose=True)
            for (r0, rg) in rgs:
                rr0 = h0 + r0
                g0 = rr0 // 512
                off = rr0 % 512
                gw = 256 if g0 == 5 else 512
                base = 128 * NF * 512 * g0 if g0 <= 5 else None
                blk = d.wtt_gu[u][base:base + 128 * NF * gw] \
                    .rearrange("(pp k c) -> pp k c", pp=128, k=NF)
                take = min(rg, gw - off)
                b.dma("qa", blk[:, k, off:off + take], bn[:, r0:r0 + take],
                      reads=[bn.res], sem=wsem)
                if take < rg:
                    g1 = g0 + 1
                    gw1 = 256 if g1 == 5 else 512
                    base1 = 128 * NF * 512 * g1
                    blk1 = d.wtt_gu[u][base1:base1 + 128 * NF * gw1] \
                        .rearrange("(pp k c) -> pp k c", pp=128, k=NF)
                    b.dma("qa", blk1[:, k, 0:rg - take],
                          bn[:, r0 + take:r0 + rg],
                          reads=[bn.res], sem=wsem)
    for k in range(NI):
        bn = p.bounce[d.rrb % 2]
        d.rrb += 1
        b.dma("qs", bn[:, 0:D], d.sdn[u][0:D, k * 128:(k + 1) * 128],
              writes=[bn.res], reads=[d.castd_res[u]], sem=bn.dsem, transpose=True)
        for og in range(4):
            blk = d.wtt_dn[u][128 * NI * og * 512:128 * NI * (og + 1) * 512]                 .rearrange("(pp k c) -> pp k c", pp=128, k=NI)
            b.dma("qa", blk[:, k, :], bn[:, og * 512:(og + 1) * 512],
                  reads=[bn.res], sem=wsem)
    d.wtt_res[u].w = (wsem, wsem.n)


def emit_pre_attn(b, d, p):
    specs = [("wq", d.wq_bf, d.wtt_wq, 4), ("wo", d.wo_bf, d.wtt_wo, 4),
             ("wk", d.wk_bf, d.wtt_wk, 1), ("wv", d.wv_bf, d.wtt_wv, 1)]
    for nm, bf, wtt, nog in specs:
        wsem = d.wtta_sem[nm]
        for og in range(nog):
            for k in range(NF):
                bn = p.bounce[d.rrb % 2]
                d.rrb += 1
                b.dma("qs", bn[:, 0:512],
                      bf[og * 512:(og + 1) * 512, k * 128:(k + 1) * 128],
                      writes=[bn.res], reads=[d.casta_res[nm]],
                      sem=bn.dsem, transpose=True)
                blk = wtt[og * 128 * NF * 512:(og + 1) * 128 * NF * 512]                     .rearrange("(pp k c) -> pp k c", pp=128, k=NF)
                b.dma("qa", blk[:, k, :], bn[:, 0:512],
                      reads=[bn.res], sem=wsem)
        d.wtta_res[nm].w = (wsem, wsem.n)


def emit_ln(b, d, p, f, src_ap, src_res, tc, do_router, xln_store=None):
    t0 = tc * TS
    sxb, sxxb = p.pb[0], p.pb[1]
    sx_waits = sxb.res.wr_waits()
    sxx_waits = sxxb.res.wr_waits()
    r_waits = [p.pb[2 + i].res.wr_waits() for i in range(4)] if do_router else None

    pe = b.pe
    cast_ms = [None] * NF
    for fc in range(NF):
        st = p.xst[fc % 2]
        b.dma("qs", st[:, :], src_ap[fc * 128:(fc + 1) * 128, t0:t0 + TS],
              writes=[st.res], reads=[src_res], sem=st.dsem)
        cast_ms[fc] = b.vec(
            lambda e, fc=fc, st=st: e.tensor_copy(
                f.xb[:, fc * TS:(fc + 1) * TS], st[:, :]),
            writes=[f.xb.res] if fc == 0 else (), reads=[st.res])
        if fc > 0:
            f.xb.res.w = cast_ms[fc]
        x2 = f.x2[fc % 2]
        b.act(lambda e, fc=fc, x2=x2: e.activation(
            x2[:, :], f.xb[:, fc * TS:(fc + 1) * TS], AF.Square),
            writes=[x2.res], extra=[cast_ms[fc]])
        ms_sx = (pe, pe.inc(1))
        b.emit("tensor",
               lambda e, fc=fc: e.matmul(sxb[0:1, :], p.ones128b[:, :],
                                         f.xb[:, fc * TS:(fc + 1) * TS],
                                         start=(fc == 0), stop=(fc == NF - 1)),
               (sx_waits + [cast_ms[fc]] + p.ones128b.res.rd_waits())
               if fc == 0 else [cast_ms[fc]],
               (pe, 1))
        ms_sxx = (pe, pe.inc(1))
        b.emit("tensor",
               lambda e, fc=fc, x2=x2: e.matmul(sxxb[0:1, :], p.ones128b[:, :],
                                                x2[:, :],
                                                start=(fc == 0), stop=(fc == NF - 1)),
               (sxx_waits if fc == 0 else []) + [x2.res.w],
               (pe, 1))
        x2.res.rd_set(ms_sxx)
        if do_router:
            for tb in range(4):
                ms_r = (pe, pe.inc(1))
                wt = []
                if fc == 0:
                    wt += r_waits[tb] + p.wrT.res.rd_waits()
                wt += [st.res.w]
                b.emit("tensor",
                       lambda e, fc=fc, tb=tb, st=st: e.matmul(
                           p.pb[2 + tb][0:128, 0:9],
                           st[:, tb * 128:(tb + 1) * 128],
                           p.wrT[:, fc * 9:(fc + 1) * 9],
                           start=(fc == 0), stop=(fc == NF - 1)),
                       wt, (pe, 1))
                st.res.rd_set(ms_r)
        else:
            st.res.rd_set(ms_sx)
    stats_end = (pe, pe.n)
    sxb.res.wr_set(stats_end)
    sxxb.res.wr_set(stats_end)
    if do_router:
        for i in range(4):
            p.pb[2 + i].res.wr_set(stats_end)
        p.wrT.res.rd_set(stats_end)
    p.ones128b.res.rd_set(stats_end)
    f.xb.res.rd_set(stats_end)

    b.vec(lambda e: e.tensor_scalar(p.negm[:, :], sxb[0:1, :], -1.0 / D, None, ALU.mult),
          writes=[p.negm.res], reads=[sxb.res])
    b.vec(lambda e: e.tensor_mul(p.msq[:, :], p.negm[:, :], p.negm[:, :]),
          writes=[p.msq.res], reads=[p.negm.res])
    b.vec(lambda e: e.scalar_tensor_tensor(p.var_r[:, :], sxxb[0:1, :], 1.0 / D,
                                           p.msq[:, :], ALU.mult, ALU.subtract),
          writes=[p.var_r.res], reads=[sxxb.res, p.msq.res])
    b.act(lambda e: e.activation(p.var_r[:, :], p.var_r[:, :], AF.Ln,
                                 bias=p.eps_t[0:1, 0:1]),
          writes=[p.var_r.res], reads=[p.eps_t.res])
    b.act(lambda e: e.activation(p.rstd[:, :], p.var_r[:, :], AF.Exp, scale=-0.5),
          writes=[p.rstd.res], reads=[p.var_r.res])
    b.vec(lambda e: e.tensor_mul(p.sneg[:, :], p.negm[:, :], p.rstd[:, :]),
          writes=[p.sneg.res], reads=[p.negm.res, p.rstd.res])
    for (row, bc) in ((p.rstd, p.rstd_b), (p.sneg, p.sneg_b)):
        bank = p.pb[0] if bc is p.rstd_b else p.pb[1]
        b.mm_group(bank, bank[0:128, :], [(p.onesrowf[0:1, :], row[0:1, :])],
                   reads=[p.onesrowf.res, row.res])
        b.vec(lambda e, bc=bc, bank=bank: e.tensor_copy(bc[:, :], bank[0:128, :]),
              writes=[bc.res], reads=[bank.res])
    for fc in range(NF):
        tmp = f.tmp[fc % 2]
        b.vec(lambda e, fc=fc, tmp=tmp: e.tensor_mul(
            tmp[:, :], f.xb[:, fc * TS:(fc + 1) * TS], p.rstd_b[:, :]),
            writes=[tmp.res], reads=[f.xb.res, p.rstd_b.res])
        xms = b.vec(lambda e, fc=fc, tmp=tmp: e.tensor_add(
            f.xln[:, fc * TS:(fc + 1) * TS], tmp[:, :], p.sneg_b[:, :]),
            writes=[f.xln.res] if fc == 0 else (),
            reads=[tmp.res, p.sneg_b.res])
        if fc > 0:
            f.xln.res.w = xms
        if xln_store is not None:
            ap, st_res, st_sem = xln_store
            b.dma("qa", ap[fc * 128:(fc + 1) * 128, t0:t0 + TS],
                  f.xln[:, fc * TS:(fc + 1) * TS],
                  reads=[f.xln.res], extra=[xms], sem=st_sem)
            st_res.w = (st_sem, st_sem.n)

    if not do_router:
        return

    for tb in range(4):
        for (row, col) in ((p.negm, 2 * tb), (p.rstd, 2 * tb + 1)):
            bank = p.pb[0] if col % 2 == 0 else p.pb[1]
            b.transpose(bank, bank[0:128, 0:1],
                        row[0:1, tb * 128:(tb + 1) * 128],
                        p.identf[0:1, 0:1], reads=[row.res, p.identf.res])
            b.vec(lambda e, col=col, bank=bank: e.tensor_copy(
                p.mr[:, col:col + 1], bank[0:128, 0:1]),
                writes=[p.mr.res], reads=[bank.res])
        rbank = p.pb[2 + tb]
        nmc = p.mr[:, 2 * tb:2 * tb + 1]
        rsc = p.mr[:, 2 * tb + 1:2 * tb + 2]
        b.vec(lambda e, rbank=rbank, nmc=nmc: e.scalar_tensor_tensor(
            p.t9[:, :], p.rb[:, :], nmc, rbank[0:128, 0:9], ALU.mult, ALU.add),
            writes=[p.t9.res], reads=[p.rb.res, p.mr.res, rbank.res])
        b.vec(lambda e, rsc=rsc: e.tensor_scalar(
            p.l9[:, :], p.t9[:, :], rsc, None, ALU.mult),
            writes=[p.l9.res], reads=[p.t9.res, p.mr.res])
        l8 = p.l9[:, 0:8]
        b.vec(lambda e: e.reduce_max(p.m1[:, :], l8, axis=AX),
              writes=[p.m1.res], reads=[p.l9.res])
        b.vec(lambda e: e.reduce_max(p.nm1[:, :], l8, axis=AX, negate=True),
              writes=[p.nm1.res], reads=[p.l9.res])
        b.vec(lambda e: e.tensor_scalar(p.eq8[:, :], l8, p.m1[:, 0:1], None,
                                        ALU.is_equal),
              writes=[p.eq8.res], reads=[p.l9.res, p.m1.res])
        b.vec(lambda e: e.scalar_tensor_tensor(p.t28[:, :], p.eq8[:, :], RBIG,
                                               l8, ALU.mult, ALU.add),
              writes=[p.t28.res], reads=[p.eq8.res, p.l9.res])
        b.vec(lambda e: e.reduce_max(p.m2[:, :], p.t28[:, :], axis=AX),
              writes=[p.m2.res], reads=[p.t28.res])
        b.act(lambda e: e.activation(p.t38[:, :], l8, AF.Exp, bias=p.nm1[:, 0:1]),
              writes=[p.t38.res], reads=[p.l9.res, p.nm1.res])
        b.act(lambda e: e.activation(p.dex[:, :], p.m2[:, :], AF.Exp,
                                     bias=p.nm1[:, 0:1]),
              writes=[p.dex.res], reads=[p.m2.res, p.nm1.res])
        b.vec(lambda e: e.tensor_scalar(p.den[:, :], p.dex[:, :], 1.0, None, ALU.add),
              writes=[p.den.res], reads=[p.dex.res])
        b.vec(lambda e: e.reciprocal(p.rden[:, :], p.den[:, :]),
              writes=[p.rden.res], reads=[p.den.res])
        b.vec(lambda e: e.tensor_scalar(p.keep[:, :], l8, p.m2[:, 0:1], None,
                                        ALU.is_ge),
              writes=[p.keep.res], reads=[p.l9.res, p.m2.res])
        b.vec(lambda e: e.tensor_mul(p.t38[:, :], p.t38[:, :], p.keep[:, :]),
              writes=[p.t38.res], reads=[p.keep.res])
        b.vec(lambda e: e.tensor_scalar(p.cg[:, 0:8], p.t38[:, :], p.rden[:, 0:1],
                                        None, ALU.mult),
              writes=[p.cg.res], reads=[p.t38.res, p.rden.res])
        b.act(lambda e: e.activation(p.dex[:, :], p.l9[:, 8:9], AF.Exp, scale=-1.0),
              writes=[p.dex.res], reads=[p.l9.res])
        b.vec(lambda e: e.tensor_scalar(p.den[:, :], p.dex[:, :], 1.0, None, ALU.add),
              writes=[p.den.res], reads=[p.dex.res])
        gms = b.vec(lambda e: e.reciprocal(p.cg[:, 8:9], p.den[:, 0:1]),
                    reads=[p.den.res])
        p.cg.res.w = gms
        bank = p.pb[2 + tb]
        b.transpose(bank, bank[0:9, 0:128], p.cg[:, 0:9], p.identf[:, :],
                    reads=[p.cg.res, p.identf.res])
        b.vec(lambda e, tb=tb, bank=bank: e.tensor_copy(
            p.cgT[0:9, tb * 128:(tb + 1) * 128], bank[0:9, 0:128]),
            writes=[p.cgT.res], reads=[bank.res])


def emit_unit(b, d, p, f, u, tc, rr, first=False):
    cb = f.cb[rr.cb % 2]
    rr.cb += 1
    sel = u if u < E else 8
    bank = p.pb[5]
    b.mm_group(bank, bank[0:128, :],
               [(p.selb[0:9, sel * 128:(sel + 1) * 128], p.cgT[0:9, :])],
               reads=[p.selb.res, p.cgT.res])
    b.vec(lambda e, cb=cb, bank=bank: e.tensor_copy(cb[:, :], bank[0:128, :]),
          writes=[cb.res], reads=[bank.res])

    hp = f.hp[rr.hp % 2]
    rr.hp += 1
    groups = [(0, 512), (512, 512), (1024, 512), (1536, 512), (2048, 512),
              (2560, 256)]
    for gi, (r0, rsz) in enumerate(groups):
        gbuf = f.gub[rr.gu % 2]
        rr.gu += 1
        blk = d.wtt_gu[u][128 * NF * r0:128 * NF * (r0 + rsz)] \
            .rearrange("(pp k c) -> pp k c", pp=128, k=NF)
        if rsz == 512:
            b.dma("qs", gbuf[:, 0:NF * 512],
                  blk.rearrange("pp k c -> pp (k c)"),
                  writes=[gbuf.res], reads=[d.wtt_res[u]], sem=gbuf.dsem)
        else:
            dstv = gbuf.t[:, 0:NF * 512].rearrange(
                "pp (k c) -> pp k c", k=NF)[:, :, 0:rsz]
            b.dma("qs", dstv, blk,
                  writes=[gbuf.res], reads=[d.wtt_res[u]], sem=gbuf.dsem)
        npair = rsz // 256
        for jl in range(npair):
            j = r0 // 256 + jl
            gb_ = p.pb[rr.pb % 4]
            ub_ = p.pb[(rr.pb + 1) % 4]
            rr.pb += 2
            b.mm_group(gb_, gb_[0:128, :],
                       [(gbuf[:, k * 512 + jl * 256:k * 512 + jl * 256 + 128],
                         f.xln[:, k * TS:(k + 1) * TS]) for k in range(NF)],
                       reads=[gbuf.res, f.xln.res])
            sl = f.silu[rr.si % 2]
            rr.si += 1
            b.act(lambda e, sl=sl, gb_=gb_: e.activation(sl[:, :], gb_[0:128, :],
                                                         AF.Silu),
                  writes=[sl.res], reads=[gb_.res])
            b.mm_group(ub_, ub_[0:128, :],
                       [(gbuf[:, k * 512 + jl * 256 + 128:k * 512 + jl * 256 + 256],
                         f.xln[:, k * TS:(k + 1) * TS]) for k in range(NF)],
                       reads=[gbuf.res, f.xln.res])
            t1 = f.t1[rr.t1 % 2]
            rr.t1 += 1
            b.vec(lambda e, t1=t1, sl=sl, ub_=ub_: e.tensor_mul(
                t1[:, :], sl[:, :], ub_[0:128, :]),
                writes=[t1.res], reads=[sl.res, ub_.res])
            hms = b.vec(lambda e, j=j, t1=t1, cb=cb, hp=hp: e.tensor_mul(
                hp[:, j * 512:(j + 1) * 512], t1[:, :], cb[:, :]),
                writes=[hp.res] if j == 0 else (), reads=[t1.res, cb.res])
            if j > 0:
                hp.res.w = hms
    for og in range(4):
        dbuf = f.dnb[rr.dn % 2]
        rr.dn += 1
        blk = d.wtt_dn[u][128 * NI * og * 512:128 * NI * (og + 1) * 512] \
            .rearrange("(pp m) -> pp m", pp=128)
        b.dma("qs", dbuf[:, 0:NI * 512], blk,
              writes=[dbuf.res], reads=[d.wtt_res[u]], sem=dbuf.dsem)
        for fo in range(4):
            fc = og * 4 + fo
            db_ = p.pb[4 + (rr.pd % 2) * 2]
            rr.pd += 1
            b.mm_group(db_, db_[0:128, :],
                       [(dbuf[:, k * 512 + fo * 128:k * 512 + fo * 128 + 128],
                         hp[:, k * 512:(k + 1) * 512]) for k in range(NI)],
                       reads=[dbuf.res, hp.res])
            oslice = f.out_acc[:, fc * TS:(fc + 1) * TS]
            ores = f.out_res[fc]
            if first:
                b.vec(lambda e, oslice=oslice, db_=db_: e.tensor_copy(
                    oslice, db_[0:128, :]),
                    writes=[ores], reads=[db_.res])
            else:
                b.vec(lambda e, oslice=oslice, db_=db_: e.tensor_add(
                    oslice, oslice, db_[0:128, :]),
                    writes=[ores], reads=[db_.res])


_PHASE_N = [0]


def emit_ffn_phase(b, d, p, cfg, src_ap, src_res, dst_ap, dst_res, pre_cb=None):
    nc = b.nc
    _PHASE_N[0] += 1
    pfx = f"f{_PHASE_N[0]}"
    with contextlib.ExitStack() as es:
        f = NS()

        def sb(name, shape, dt):
            return Buf(es.enter_context(nc.sbuf_tensor(pfx + name, shape, dt)))
        f.xb = sb("ffn_xb", [128, NF * TS], BF16)
        f.xln = sb("ffn_xln", [128, NF * TS], BF16)
        f.x2 = [sb("ffn_x2a", [128, TS], BF16), sb("ffn_x2b", [128, TS], BF16)]
        f.tmp = [sb("ffn_tmpa", [128, TS], F32), sb("ffn_tmpb", [128, TS], F32)]
        f.out_acc = sb("ffn_out", [128, NF * TS], F32)
        f.out_res = [Res() for _ in range(NF)]
        f.gub = [sb("ffn_guba", [128, NF * 512], BF16),
                 sb("ffn_gubb", [128, NF * 512], BF16)]
        f.dnb = [sb("ffn_dnba", [128, NI * 512], BF16),
                 sb("ffn_dnbb", [128, NI * 512], BF16)]
        for i in (0, 1):
            f.gub[i].dsem = b.mksem(f"d_gub{i}")
            f.dnb[i].dsem = b.mksem(f"d_dnb{i}")
        f.hp = [sb("ffn_hpa", [128, NI * 512], BF16),
                sb("ffn_hpb", [128, NI * 512], BF16)]
        f.silu = [sb("ffn_sila", [128, 512], F32), sb("ffn_silb", [128, 512], F32)]
        f.t1 = [sb("ffn_t1a", [128, 512], F32), sb("ffn_t1b", [128, 512], F32)]
        f.cb = [sb("ffn_cba", [128, 512], F32), sb("ffn_cbb", [128, 512], F32)]

        rr = NS()
        rr.cb = rr.hp = rr.gu = rr.si = rr.t1 = rr.dn = rr.pb = rr.pd = 0
        units = cfg.get("units", list(range(NU)))
        for tc in range(cfg["nt"]):
            emit_ln(b, d, p, f, src_ap, src_res, tc, do_router=True)
            if cfg.get("dump_xln"):
                for fc in range(NF):
                    b.dma("qa", d.dbg_xln[fc * 128:(fc + 1) * 128,
                                          tc * TS:(tc + 1) * TS],
                          f.xln[:, fc * TS:(fc + 1) * TS], reads=[f.xln.res],
                          sem=p.dbgsem)
            if cfg.get("dump_cgt"):
                b.dma("qa", d.dbg_cgt[0:9, tc * TS:(tc + 1) * TS],
                      p.cgT[0:9, :], reads=[p.cgT.res], sem=p.dbgsem)
            for ui, u in enumerate(units):
                if tc == 0 and pre_cb is not None:
                    pre_cb(u)
                emit_unit(b, d, p, f, u, tc, rr, first=(ui == 0))
            if tc == 0 and pre_cb is not None:
                pre_cb("attn")
            t0 = tc * TS
            for fc in range(NF):
                st = p.xst[fc % 2]
                b.dma("qs", st[:, :], src_ap[fc * 128:(fc + 1) * 128, t0:t0 + TS],
                      writes=[st.res], reads=[src_res], sem=st.dsem)
                ys = p.yst[fc % 2]
                b.vec(lambda e, fc=fc, st=st, ys=ys: e.scalar_tensor_tensor(
                    ys[:, :], f.out_acc[:, fc * TS:(fc + 1) * TS], 0.5,
                    st[:, :], ALU.mult, ALU.add),
                    writes=[ys.res], reads=[f.out_res[fc], st.res])
                b.dma("qa", dst_ap[fc * 128:(fc + 1) * 128, t0:t0 + TS], ys[:, :],
                      reads=[ys.res], sem=ys.dsem)
                dst_res[tc].w = (ys.dsem, ys.dsem.n)


def emit_attn_ln(b, d, p, cfg, src_ap, src_res):
    nc = b.nc
    with contextlib.ExitStack() as es:
        f = NS()

        def sb(name, shape, dt):
            return Buf(es.enter_context(nc.sbuf_tensor(name, shape, dt)))
        f.xb = sb("aln_xb", [128, NF * TS], BF16)
        f.xln = sb("aln_xln", [128, NF * TS], BF16)
        f.x2 = [sb("aln_x2a", [128, TS], BF16), sb("aln_x2b", [128, TS], BF16)]
        f.tmp = [sb("aln_tmpa", [128, TS], F32), sb("aln_tmpb", [128, TS], F32)]
        d.xln_res = Res()
        xsem = b.mksem("d_xlnst")
        for tc in range(cfg["nt"]):
            emit_ln(b, d, p, f, src_ap, src_res[tc], tc, do_router=False,
                    xln_store=(d.xln_fm, d.xln_res, xsem))


def emit_attn_seq(b, d, p, cfg, sq, src_res, dst_res, f):
    base = sq * SEQ
    nq = cfg["nt"] * TS // SEQ if cfg["nt"] < NTC else 4

    for wi, nm in enumerate(("wk", "wv")):
        buf = f.wstream[wi]
        wtt = d.wtt_wk if nm == "wk" else d.wtt_wv
        b.dma("qs", buf[:, 0:NF * 512],
              wtt[:].rearrange("(pp m) -> pp m", pp=128),
              writes=[buf.res], reads=[d.wtta_res[nm]], sem=buf.dsem)
    for kc in range(4):
        for k in range(NF):
            b.dma("qs", f.xlnt[:, k * 512:(k + 1) * 512],
                  d.xln_fm[k * 128:(k + 1) * 128, base + kc * 512:base + (kc + 1) * 512],
                  writes=[f.xlnt.res] if k == 0 else (),
                  reads=[d.xln_res] if k == 0 else (), sem=f.xlnt.dsem)
            if k > 0:
                f.xlnt.res.w = (f.xlnt.dsem, f.xlnt.dsem.n)
        for oc in range(4):
            bank = p.pb[5 + (oc % 2)]
            b.mm_group(bank, bank[0:128, :],
                       [(f.wstream[0][:, k * 512 + oc * 128:k * 512 + oc * 128 + 128],
                         f.xlnt[:, k * 512:(k + 1) * 512]) for k in range(NF)],
                       reads=[f.wstream[0].res, f.xlnt.res])
            b.vec(lambda e, kc=kc, oc=oc, bank=bank: e.tensor_copy(
                f.kT[:, oc * 2048 + kc * 512:oc * 2048 + (kc + 1) * 512],
                bank[0:128, :]),
                writes=[f.kT.res], reads=[bank.res])
        for oc in range(4):
            bank = p.pb[5 + (oc % 2)]
            b.mm_group(bank, bank[0:128, :],
                       [(f.wstream[1][:, k * 512 + oc * 128:k * 512 + oc * 128 + 128],
                         f.xlnt[:, k * 512:(k + 1) * 512]) for k in range(NF)],
                       reads=[f.wstream[1].res, f.xlnt.res])
            vs = f.vstage[oc % 2]
            b.vec(lambda e, vs=vs, bank=bank: e.tensor_copy(vs[:, :], bank[0:128, :]),
                  writes=[vs.res], reads=[bank.res])
            for t in range(4):
                cc = kc * 4 + t
                b.transpose(p.pbT, p.pbT[0:128, 0:128], vs[:, t * 128:(t + 1) * 128],
                            p.ident[:, :], reads=[vs.res, p.ident.res])
                for gg in range(2):
                    g = 2 * oc + gg
                    b.vec(lambda e, cc=cc, g=g, gg=gg: e.tensor_copy(
                        f.vtm[:, cc * 520 + g * 65:cc * 520 + g * 65 + 64],
                        p.pbT[0:128, gg * 64:(gg + 1) * 64]),
                        writes=[f.vtm.res], reads=[p.pbT.res])
    for cc in range(16):
        for g in range(G):
            b.vec(lambda e, cc=cc, g=g: e.memset(
                f.vtm[:, cc * 520 + g * 65 + 64:cc * 520 + g * 65 + 65], 1.0),
                writes=[f.vtm.res])

    for qpc in range(nq):
        q0 = base + qpc * 512
        for k in range(NF):
            b.dma("qs", f.xlnt[:, k * 512:(k + 1) * 512],
                  d.xln_fm[k * 128:(k + 1) * 128, q0:q0 + 512],
                  writes=[f.xlnt.res] if k == 0 else (),
                  reads=[d.xln_res] if k == 0 else (), sem=f.xlnt.dsem)
            if k > 0:
                f.xlnt.res.w = (f.xlnt.dsem, f.xlnt.dsem.n)
        for og in range(4):
            buf = f.wstream[og % 2]
            b.dma("qs", buf[:, 0:NF * 512],
                  d.wtt_wq[og * 128 * NF * 512:(og + 1) * 128 * NF * 512]
                  .rearrange("(pp m) -> pp m", pp=128),
                  writes=[buf.res], reads=[d.wtta_res["wq"]], sem=buf.dsem)
            for j in range(4):
                bank = p.pb[5 + (j % 2)]
                b.mm_group(bank, bank[0:128, :],
                           [(buf[:, k * 512 + j * 128:k * 512 + j * 128 + 128],
                             f.xlnt[:, k * 512:(k + 1) * 512]) for k in range(NF)],
                           reads=[buf.res, f.xlnt.res])
                ci = og * 4 + j
                b.vec(lambda e, ci=ci, bank=bank: e.tensor_copy(
                    f.qT[:, ci * 512:(ci + 1) * 512], bank[0:128, :]),
                    writes=[f.qT.res], reads=[bank.res])
        for g in range(G):
            for j in range(4):
                par = (4 * g + j) % 2
                odd = (j % 2) == 1
                kmax = 4 * qpc + 4
                koff = (g % 2) * 64
                kcol = (g // 2) * 2048
                qoff = (g % 2) * 64
                qcol = ((g // 2) * 4 + j) * 512
                coff = (j % 2) * 64
                ccol = (2 * g + j // 2) * 512
                wT = f.wT[par]
                ctxb = p.pb[3 + par]
                zb = p.pb[2]
                exp_ms = [None] * kmax
                for kpc in range(kmax):
                    sb_ = p.pb[kpc % 2]
                    ms_s = b.mm_group(
                        sb_, sb_[0:128, :],
                        [(f.kT[koff:koff + 64, kcol + kpc * 128:kcol + (kpc + 1) * 128],
                          f.qT[qoff:qoff + 64, qcol:qcol + 512])],
                        reads=[f.kT.res, f.qT.res])
                    wsl = wT[:, kpc * 512:(kpc + 1) * 512]
                    ems = b.act(lambda e, wsl=wsl, sb_=sb_: e.activation(
                        wsl, sb_[0:128, :], AF.Exp, bias=p.negC[:, 0:1],
                        scale=0.125),
                        writes=[wT.res] if kpc == 0 else (),
                        reads=[sb_.res, p.negC.res])
                    if kpc > 0:
                        wT.res.w = ems
                    dd = 512 * qpc - 128 * kpc
                    if kpc >= 4 * qpc:
                        mms = b.vec(lambda e, wsl=wsl, dd=dd: e.tensor_mul(
                            wsl, wsl, p.maskT[:, dd + 384:dd + 384 + 512]),
                            reads=[p.maskT.res], extra=[ems])
                        wT.res.w = mms
                        ems = mms
                    exp_ms[kpc] = ems
                vw = 65 if not odd else 64
                ctx_out = ctxb[0:65, :] if not odd else ctxb[64:128, :]
                tp = None if not odd else (0, 64)
                mms_list = []
                for kpc in range(kmax):
                    lh = f.vtm[:, kpc * 520 + g * 65:kpc * 520 + g * 65 + vw]
                    rh = wT[:, kpc * 512:(kpc + 1) * 512]
                    mms_list.append((lh, rh))
                gms = (b.pe, b.pe.inc(1))
                w0 = ctxb.res.wr_waits() + f.vtm.res.rd_waits()
                for kpc in range(kmax):
                    lh, rh = mms_list[kpc]
                    b.emit("tensor",
                           lambda e, o=ctx_out, l=lh, r=rh, st=(kpc == 0),
                                  sp=(kpc == kmax - 1), tp=tp:
                           e.matmul(o, l, r, start=st, stop=sp, tile_position=tp),
                           (w0 if kpc == 0 else []) + [exp_ms[kpc]],
                           (b.pe, 1) if kpc == kmax - 1 else None)
                ctxb.res.wr_set(gms)
                f.vtm.res.rd_set(gms)
                wT.res.rd_set(gms)
                if odd:
                    zms = (b.pe, b.pe.inc(1))
                    wz = zb.res.wr_waits()
                    for kpc in range(kmax):
                        rh = wT[:, kpc * 512:(kpc + 1) * 512]
                        b.emit("tensor",
                               lambda e, r=rh, st=(kpc == 0), sp=(kpc == kmax - 1):
                               e.matmul(zb[0:1, :], p.ones128b[:, :], r,
                                        start=st, stop=sp),
                               (wz if kpc == 0 else []) + [exp_ms[kpc]],
                               (b.pe, 1) if kpc == kmax - 1 else None)
                    zb.res.wr_set(zms)
                    wT.res.rd_set(zms)
                rzt = f.rz[par]
                zsrc = ctxb[64:65, :] if not odd else zb[0:1, :]
                zrow = 64 if not odd else 0
                b.vec(lambda e, rzt=rzt, zsrc=zsrc, zrow=zrow: e.reciprocal(
                    rzt[zrow:zrow + 1, :], zsrc),
                    writes=[rzt.res], reads=[ctxb.res if not odd else zb.res],
                    lowp=True)
                bank = p.pb[5]
                b.mm_group(bank, bank[0:128, :],
                           [(p.onesrowb[zrow:zrow + 1, :], rzt[zrow:zrow + 1, :])],
                           reads=[p.onesrowb.res, rzt.res])
                rzb = f.rzb[par]
                b.vec(lambda e, rzb=rzb, bank=bank: e.tensor_copy(
                    rzb[:, :], bank[0:128, :]),
                    writes=[rzb.res], reads=[bank.res])
                crange = (0, 64) if not odd else (64, 128)
                b.vec(lambda e, ccol=ccol, coff=coff, rzb=rzb, crange=crange,
                             ctxb=ctxb: e.tensor_mul(
                    f.ctxT[coff:coff + 64, ccol:ccol + 512],
                    ctxb[crange[0]:crange[1], :],
                    rzb[crange[0]:crange[1], :]),
                    writes=[f.ctxT.res], reads=[ctxb.res, rzb.res])
        for og in range(4):
            buf = f.wstream[og % 2]
            b.dma("qs", buf[:, 0:NF * 512],
                  d.wtt_wo[og * 128 * NF * 512:(og + 1) * 128 * NF * 512]
                  .rearrange("(pp m) -> pp m", pp=128),
                  writes=[buf.res], reads=[d.wtta_res["wo"]], sem=buf.dsem)
            for oc in range(4):
                fc = og * 4 + oc
                bank = p.pb[5 + (oc % 2)]
                b.mm_group(bank, bank[0:128, :],
                           [(buf[:, k * 512 + oc * 128:k * 512 + oc * 128 + 128],
                             f.ctxT[:, k * 512:(k + 1) * 512]) for k in range(NF)],
                           reads=[buf.res, f.ctxT.res])
                st = p.xst[oc % 2]
                tcg = (base + qpc * 512) // TS
                b.dma("qs", st[:, :],
                      d.attn_src[fc * 128:(fc + 1) * 128, q0:q0 + 512],
                      writes=[st.res], reads=[src_res[tcg]], sem=st.dsem)
                ys = p.yst[oc % 2]
                b.vec(lambda e, st=st, ys=ys, bank=bank: e.tensor_add(
                    ys[:, :], st[:, :], bank[0:128, :]),
                    writes=[ys.res], reads=[st.res, bank.res])
                b.dma("qa", d.attn_dst[fc * 128:(fc + 1) * 128, q0:q0 + 512],
                      ys[:, :], reads=[ys.res], sem=ys.dsem)
                dst_res[tcg].w = (ys.dsem, ys.dsem.n)


def emit_attn_phase(b, d, p, cfg, src_ap, src_res, dst_ap, dst_res):
    nc = b.nc
    d.attn_src = src_ap
    d.attn_dst = dst_ap
    emit_attn_ln(b, d, p, cfg, src_ap, src_res)
    b.barrier()
    with contextlib.ExitStack() as es:
        f = NS()

        def sb(name, shape, dt):
            return Buf(es.enter_context(nc.sbuf_tensor(name, shape, dt)))
        f.xlnt = sb("at_xlnt", [128, NF * 512], BF16)
        f.xlnt.dsem = b.mksem("d_xlnt")
        f.kT = sb("at_kT", [128, 4 * 2048], BF16)
        f.vtm = sb("at_vtm", [128, 16 * 520], BF16)
        f.qT = sb("at_qT", [128, NF * 512], BF16)
        f.wstream = [sb("at_wsa", [128, NF * 512], BF16),
                     sb("at_wsb", [128, NF * 512], BF16)]
        f.wstream[0].dsem = b.mksem("d_wsa")
        f.wstream[1].dsem = b.mksem("d_wsb")
        f.wT = [sb("at_wTa", [128, 16 * 512], BF16),
                sb("at_wTb", [128, 16 * 512], BF16)]
        f.ctxT = sb("at_ctxT", [128, NF * 512], BF16)
        f.vstage = [sb("at_vsa", [128, 512], BF16), sb("at_vsb", [128, 512], BF16)]
        f.rz = [sb("at_rza", [65, 512], BF16), sb("at_rzb", [65, 512], BF16)]
        f.rzb = [sb("at_rzba", [128, 512], F32), sb("at_rzbb", [128, 512], F32)]
        nseq = cfg.get("nseq", 2)
        for sq in range(nseq):
            emit_attn_seq(b, d, p, cfg, sq, src_res, dst_res, f)


DEFAULT_CFG = dict(phases=("pre", "ffn1", "attn", "ffn2"), nt=NTC, nseq=2,
                   ffn1_dst="y1", attn_dst="y2")


def build_program(cfg=None):
    cfg = {**DEFAULT_CFG, **(cfg or {})}
    nc = bass.Bass("TRN2", target_bir_lowering=False)
    d = declare_drams(nc)
    b = B(nc)
    dmap = {"y1": d.y1, "y2": d.y2, "out": d.out}
    x_res = [Res() for _ in range(NTC)]
    y1_res = [Res() for _ in range(NTC)]
    y2_res = [Res() for _ in range(NTC)]
    out_res = [Res() for _ in range(NTC)]
    rmap = {"y1": y1_res, "y2": y2_res, "out": out_res}
    x_res_c = Res()

    with contextlib.ExitStack() as es:
        p = alloc_persistent(b, es)
        if "pre" in cfg["phases"]:
            emit_preprocess(b, d, p, cfg)
        if "ffn1" in cfg["phases"]:
            def pre_cb(u):
                if u == "attn":
                    if "attn" in cfg["phases"]:
                        emit_pre_attn(b, d, p)
                else:
                    emit_pre_unit(b, d, p, u)
            emit_ffn_phase(b, d, p, cfg, d.x_fm, x_res_c,
                           dmap[cfg["ffn1_dst"]], rmap[cfg["ffn1_dst"]],
                           pre_cb=pre_cb)
        if "attn" in cfg["phases"]:
            b.barrier()
            emit_attn_phase(b, d, p, cfg, d.y1, y1_res,
                            dmap[cfg["attn_dst"]], rmap[cfg["attn_dst"]])
        if "ffn2" in cfg["phases"]:
            b.barrier()
            emit_ffn_phase(b, d, p, cfg, d.y2, Res_merge(y2_res), d.out, out_res)
        fin = [(s, s.n) for s in b.all_sems if s.n > 0]
        b.emit("gpsimd", lambda e: e.nop(), fin, None)
        b.realize(es)
    return nc


class Res_merge:
    def __init__(self, rs):
        self.rs = rs

    def rd_waits(self):
        w = []
        for r in self.rs:
            w += r.rd_waits()
        return w

    def rd_set(self, ms):
        for r in self.rs:
            r.rd_set(ms)


def host_constants():
    c = {}
    c["c_ident"] = np.eye(128).astype(ml_dtypes.bfloat16)
    c["c_identf"] = np.eye(128).astype(np.float32)
    c["c_ones128b"] = np.ones((128, 1), ml_dtypes.bfloat16)
    c["c_ones128f"] = np.ones((128, 1), np.float32)
    orow = np.zeros((65, 128), np.float32)
    orow[0, :] = 1.0
    orow[64, :] = 1.0
    c["c_onesrowf"] = orow
    c["c_onesrowb"] = orow.astype(ml_dtypes.bfloat16)
    sel = np.zeros((9, 9 * 128), np.float32)
    for e in range(9):
        sel[e, e * 128:(e + 1) * 128] = 1.0
    c["c_selb"] = sel.astype(ml_dtypes.bfloat16)
    m = (np.arange(128)[:, None] <= (np.arange(896)[None, :] - 384))
    c["c_maskT"] = m.astype(ml_dtypes.bfloat16)
    return c


WNAMES = {"wq": "L0_attn_Wq", "wk": "L0_attn_Wk", "wv": "L0_attn_Wv",
          "wo": "L0_attn_Wo", "wr": "L0_ffn_router_weight",
          "gup": "L0_ffn_gate_up_proj", "dnp": "L0_ffn_down_proj",
          "shg": "L0_ffn_sh_gate_w", "shu": "L0_ffn_sh_up_w",
          "shd": "L0_ffn_sh_down_w", "wsg": "L0_ffn_shared_gate_w"}


def kernel(**inputs):
    x = np.asarray(inputs["L0_attn_x"])
    consts = host_constants()
    weights = {k: np.ascontiguousarray(np.asarray(inputs[nm], dtype=np.float32))
               for k, nm in WNAMES.items()}
    nc = build_program()
    in_maps = []
    for c in range(8):
        m = dict(consts)
        m.update(weights)
        m["x_fm"] = np.ascontiguousarray(
            x[2 * c:2 * c + 2].transpose(2, 0, 1).reshape(D, TPC))
        in_maps.append(m)
    res = run_bass_kernel_spmd(nc, in_maps, core_ids=list(range(8)))
    parts = []
    for r in res.results:
        o = r["out_fm"]
        parts.append(o.reshape(D, 2, SEQ).transpose(1, 2, 0))
    return np.ascontiguousarray(np.concatenate(parts, 0)).astype(np.float32)
